# version 33
# baseline (speedup 1.0000x reference)
import numpy as np
import concourse.bass as bass
import concourse.mybir as mybir
from concourse.bass_utils import run_bass_kernel_spmd

F32 = mybir.dt.float32
F16 = mybir.dt.float16
BF16 = mybir.dt.bfloat16
AF = mybir.ActivationFunctionType
OP = mybir.AluOpType

NCORES = 8
D = 2048
ND = 16
T = 512
NP = 2
FF = 8192
NF = 64
TC = 256
EPS = 1e-5
INVD = 1.0 / 2048.0


def build_program():
    nc = bass.Bass("TRN2", target_bir_lowering=False, debug=False,
                   detect_race_conditions=False)

    d_x = [nc.dram_tensor(f"xin{p}", [2048, 512], F32, kind="ExternalInput").ap()
           for p in range(NP)]
    d_xb1 = [nc.dram_tensor(f"xb1{p}", [128, 16], F32, kind="ExternalInput").ap()
             for p in range(NP)]
    d_wk = nc.dram_tensor("wk", [16, 128, 2048], F32, kind="ExternalInput").ap()
    d_wv = nc.dram_tensor("wv", [16, 128, 2048], F32, kind="ExternalInput").ap()
    d_wr = nc.dram_tensor("wr", [16, 128, 2048], F32, kind="ExternalInput").ap()
    d_wo = nc.dram_tensor("wo", [16, 128, 2048], F32, kind="ExternalInput").ap()
    d_w0 = nc.dram_tensor("w0", [64, 128, 2048], F32, kind="ExternalInput").ap()
    d_w1 = nc.dram_tensor("w1", [2, 64, 128, 1024], F32, kind="ExternalInput").ap()
    d_tmk = nc.dram_tensor("tmk", [128, 16], F32, kind="ExternalInput").ap()
    d_tmv = nc.dram_tensor("tmv", [128, 16], F32, kind="ExternalInput").ap()
    d_tmr = nc.dram_tensor("tmr", [128, 16], F32, kind="ExternalInput").ap()
    d_tf = nc.dram_tensor("tf", [128, 16], F32, kind="ExternalInput").ap()
    d_lng = nc.dram_tensor("lng", [128, 16], F32, kind="ExternalInput").ap()
    d_lngr = nc.dram_tensor("lngr", [1, 2048], F32, kind="ExternalInput").ap()
    d_lnb = nc.dram_tensor("lnb", [128, 16], F32, kind="ExternalInput").ap()
    d_b1 = nc.dram_tensor("b1", [128, 16], F32, kind="ExternalInput").ap()
    d_b0 = nc.dram_tensor("b0", [128, 64], F32, kind="ExternalInput").ap()
    d_mask = nc.dram_tensor("mask", [1, 8], F32, kind="ExternalInput").ap()
    d_id = nc.dram_tensor("ident", [128, 128], F32, kind="ExternalInput").ap()
    d_out = nc.dram_tensor("out", [2048, 1024], F32, kind="ExternalOutput").ap()
    cc_in = [nc.dram_tensor(f"cc_in{p}", [128, 32], F32) for p in range(NP)]
    cc_out = [nc.dram_tensor(f"cc_out{p}", [8, 128, 32], F32) for p in range(NP)]

    A = nc.alloc_sbuf_tensor
    sv_tmk = A("sv_tmk", [128, 16], F32)
    sv_tmk1 = A("sv_tmk1", [128, 16], F32)
    sv_tmv = A("sv_tmv", [128, 16], F32)
    sv_tmr = A("sv_tmr", [128, 16], F32)
    sv_tf = A("sv_tf", [128, 16], F32)
    sv_eu = A("sv_eu", [128, 16], F32)
    sv_lng = A("sv_lng", [128, 16], F32)
    sv_lngr = A("sv_lngr", [1, 2048], F32)
    sv_lnb = A("sv_lnb", [128, 16], F32)
    sv_b1 = A("sv_b1", [128, 16], F32)
    sv_b0 = A("sv_b0", [128, 64], F32)
    sv_mask = A("sv_mask", [128, 8], F32)
    s_id = A("s_id", [128, 128], F32)
    s_onescol = A("s_onescol", [128, 1], F32)
    s_onesrow = A("s_onesrow", [1, 128], F32)
    s_statk = A("s_statk", [128, 32], F32)
    s_statv = A("s_statv", [128, 32], F32)
    s_g = A("s_g", [128, 256], F32)
    s_offs = A("s_offs", [128, 32], F32)
    s_tot = A("s_tot", [128, 32], F32)
    s_eps = A("s_eps", [1, 1], F32)
    s_pa = A("s_pa", [1, 512], F32)
    s_pb = A("s_pb", [1, 512], F32)
    s_pc = A("s_pc", [1, 512], F32)
    s_tm1 = {}
    s_G1 = A("s_G1", [128, 512], F32)
    s_S1 = A("s_S1", [128, 512], F32)
    s_S2 = A("s_S2", [128, 512], F32)

    xbuf = A("xbuf", [128, ND * 512], F32)
    xsb = A("xsb", [128, 16], F32)
    mixA = A("mixA", [128, ND * 512], F32)
    mixB = A("mixB", [128, ND * 512], F32)
    attn2 = A("attn2", [128, 2 * ND * 513], F32)
    wb = [A(f"wb{k}", [128, 2048], F32) for k in range(2)]
    aw_h = [A(f"aw_h{k}", [128, 1024], BF16) for k in range(2)]
    aw_l = [A(f"aw_l{k}", [128, 1024], BF16) for k in range(2)]

    def xb(i, c0, w):
        return xbuf[:, i * 512 + c0 - 128: i * 512 + c0 - 128 + w]

    def mA(i, c0=0, w=512):
        return mixA[:, i * 512 + c0: i * 512 + c0 + w]

    def mB(i, c0=0, w=512):
        return mixB[:, i * 512 + c0: i * 512 + c0 + w]

    def ek_t(j, c0, w):
        return attn2[:, j * 513 + c0: j * 513 + c0 + w]

    def ekv_t(j, c0, w):
        return attn2[:, ND * 513 + j * 513 + c0: ND * 513 + j * 513 + c0 + w]

    def hg_t(f):
        return attn2[:, f * TC: (f + 1) * TC]

    def stg(k):
        return mixB[:, k * 2048: (k + 1) * 2048]

    xstage = [mixA[:, k * 2048: (k + 1) * 2048] for k in range(4)]

    mixA_f16 = mixA[:].bitcast(F16)
    mixA_bf = mixA[:].bitcast(BF16)

    def xq16_t(j):
        return mixA_f16[:, j * 512:(j + 1) * 512]

    def xqb_t(j):
        return mixA_bf[:, 8192 + j * 512: 8192 + (j + 1) * 512]

    def mAh(i):
        base = 4096 * (i % 4) + 1024 * (i // 4)
        return mixA_bf[:, base: base + 512]

    def mAl(i):
        base = 4096 * (i % 4) + 1024 * (i // 4)
        return mixA_bf[:, base + 512: base + 1024]

    def mBh(i):
        return mixB_bf[:, i * 512:(i + 1) * 512]

    def mBl(i):
        return mixB_bf[:, 8192 + i * 512: 8192 + (i + 1) * 512]

    attn2_bf = attn2[:].bitcast(BF16)

    def rw_h(j):
        return attn2_bf[:, 1026 * j: 1026 * j + 512]

    def rw_l(j):
        return attn2_bf[:, 1026 * j + 512: 1026 * j + 1024]

    attn2_f16 = attn2[:].bitcast(F16)

    def hg16_t(f):
        return attn2_f16[:, f * 512:(f + 1) * 512]

    def out_t(j):
        return attn2[:, j * 512:(j + 1) * 512]

    mixB_f16 = mixB[:].bitcast(F16)
    mixB_bf = mixB[:].bitcast(BF16)
    w16 = [mixB_f16[:, k * 2048:(k + 1) * 2048] for k in range(2)]
    w16b = [mixB_bf[:, k * 2048:(k + 1) * 2048] for k in range(2)]
    wbl = [mixB_bf[:, 4096 + k * 2048: 4096 + (k + 1) * 2048] for k in range(2)]
    hgb = [mixB_bf[:, 8192 + k * 512: 8192 + (k + 1) * 512] for k in range(2)]
    s_h16 = [mixB_f16[:, 9216 + k * 512: 9216 + (k + 1) * 512] for k in range(2)]

    pb = [nc.alloc_psum_tensor(f"pb{k}", [128, 512], F32) for k in range(8)]

    sem = {nm: nc.alloc_semaphore(nm) for nm in ("L", "G", "T", "V", "C", "K", "P")}
    n = {nm: 0 for nm in sem}
    q = {e: [] for e in ("sync", "gpsimd", "tensor", "vector", "scalar")}

    def emit(eng, waits, f, inc=None, inc_n=1):
        for s, v in waits:
            assert v <= n[s], f"wait {s}>={v} but only {n[s]} emitted"
        ws = [(sem[s], v) for s, v in waits if v > 0]
        if inc is not None:
            n[inc] += inc_n

        def g(e, ws=ws, f=f, h=(sem[inc] if inc else None), hn=inc_n):
            for s, v in ws:
                e.wait_ge(s, v)
            inst = f(e)
            if h is not None:
                inst.then_inc(h, hn)
        q[eng].append(g)
        return n[inc] if inc is not None else None

    def dma(eng_q, waits, out_ap, in_ap, inc):
        return emit(eng_q, waits,
                    lambda e, o=out_ap, i=in_ap: e.dma_start(out=o, in_=i),
                    inc=inc, inc_n=16)

    for dst, src in [(sv_tmk, d_tmk), (sv_tmv, d_tmv), (sv_tmr, d_tmr),
                     (sv_tf, d_tf), (sv_lng, d_lng), (sv_lnb, d_lnb),
                     (sv_b1, d_b1), (sv_b0, d_b0), (s_id, d_id),
                     (sv_lngr, d_lngr)]:
        dma("sync", [], dst[:], src, "L")
    dma("sync", [], sv_mask[:], d_mask.broadcast_to([128, 8]), "L")
    L_consts = n["L"]
    emit("vector", [], lambda e: e.memset(s_onescol[:], 1.0), inc=None)
    emit("vector", [], lambda e: e.memset(s_eps[:], float(EPS)), inc=None)
    emit("vector", [("L", L_consts)], lambda e: e.tensor_scalar(
        sv_tmk1[:], sv_tmk[:], -1.0, None, OP.mult), inc=None)
    emit("vector", [], lambda e: e.tensor_scalar(
        sv_tmk1[:], sv_tmk1[:], 1.0, None, OP.add), inc=None)
    V_ones = emit("vector", [], lambda e: e.memset(s_onesrow[:], 1.0), inc="V")
    C_eu = emit("scalar", [("L", L_consts)],
                lambda e: e.activation(sv_eu[:], sv_tf[:], AF.Exp), inc="C")

    wb_cons = [[("T", 0)], [("T", 0)]]
    awh_cons = [("T", 0), ("T", 0)]
    awl_cons = [("T", 0), ("T", 0)]
    w16_cons = [("T", 0), ("T", 0)]
    wbl_cons = [("T", 0), ("T", 0)]
    hgb_cons = [("T", 0), ("T", 0)]
    bank_cons = [("T", 0)] * 8
    mixA_free = ("T", 0)
    mixB_free = ("T", 0)
    attn2_free = ("T", 0)
    xbuf_free = [("T", 0)] * ND
    stats_free = ("G", 0)
    h16_free = ("V", 0)
    V_tot_p0 = 0
    stg_store = {b: ("G", 0) for b in range(4)}

    def prep_wa(dram_w, j, sc, k, extra, lo_eng="gpsimd"):
        lo_sem = "P" if lo_eng == "gpsimd" else "V"
        Lv = dma("sync", wb_cons[k], wb[k][:, 0:1024],
                 dram_w[j][:, sc * 1024:(sc + 1) * 1024], "L")
        Ch = emit("scalar", [("L", Lv), awh_cons[k]] + extra,
                  lambda e, k=k: e.copy(aw_h[k][:], wb[k][:, 0:1024]),
                  inc="C")
        Vl = emit(lo_eng, [("C", Ch), awl_cons[k]],
                  lambda e, k=k: e.tensor_tensor(aw_l[k][:], wb[k][:, 0:1024],
                                                 aw_h[k][:], OP.subtract),
                  inc=lo_sem)
        wb_cons[k] = [("C", Ch), (lo_sem, Vl)]
        return (Ch, (lo_sem, Vl))

    def proj3(dram_w, rhsh_fn, rhsl_fn, consumer, extra_first, first_prep=(),
              lo_eng="gpsimd", drip=None, extra_second=()):
        out = []
        seq = [(j, sc) for j in range(ND) for sc in (0, 1)]
        prepped = {0: prep_wa(dram_w, 0, 0, 0, list(first_prep), lo_eng)}
        for idx, (j, sc) in enumerate(seq):
            Ch, Vl = prepped[idx]
            k = idx % 2
            bank = j % 2
            w0 = [("C", Ch)] + ([bank_cons[bank]] if sc == 0 else []) \
                + (extra_first if idx == 0 else []) \
                + (list(extra_second) if idx == 1 else [])
            if drip is not None and idx < len(drip):
                for th in drip[idx]:
                    th()
            i0 = sc * 8
            for q in range(8):
                emit("tensor", w0 if q == 0 else [],
                     lambda e, b=bank, k=k, q=q, i=i0 + q, sc=sc:
                     e.matmul(pb[b][:], aw_h[k][:, q * 128:(q + 1) * 128],
                              rhsh_fn(i), start=(sc == 0 and q == 0),
                              stop=False),
                     inc=None)
                emit("tensor", [],
                     lambda e, b=bank, k=k, q=q, i=i0 + q:
                     e.matmul(pb[b][:], aw_h[k][:, q * 128:(q + 1) * 128],
                              rhsl_fn(i), start=False, stop=False),
                     inc=None)
            for q in range(8):
                lastv = emit("tensor", [Vl] if q == 0 else [],
                             lambda e, b=bank, k=k, q=q, i=i0 + q, sc=sc:
                             e.matmul(pb[b][:], aw_l[k][:, q * 128:(q + 1) * 128],
                                      rhsh_fn(i), start=False,
                                      stop=(sc == 1 and q == 7)),
                             inc="T" if q == 7 else None)
            awh_cons[k] = ("T", lastv)
            awl_cons[k] = ("T", lastv)
            if idx + 1 < len(seq):
                j2, sc2 = seq[idx + 1]
                prepped[idx + 1] = prep_wa(dram_w, j2, sc2, (idx + 1) % 2, [],
                                           lo_eng)
            if sc == 1:
                ev = consumer(j, bank, lastv)
                bank_cons[bank] = ev
                out.append(ev)
        return out

    def ln_params(T_ready, width):
        emit("vector", [("T", T_ready)], lambda e: e.tensor_scalar(
            s_pa[:, 0:width], pb[4][0:1, 0:width], INVD, None, OP.mult),
            inc=None)
        Vc = emit("vector", [], lambda e: e.tensor_scalar(
            s_pb[:, 0:width], pb[5][32:33, 0:width], INVD, None, OP.mult),
            inc="V")
        bank_cons[4] = ("V", Vc)
        bank_cons[5] = ("V", Vc)
        emit("vector", [], lambda e: e.tensor_tensor(
            s_S1[0:1, 0:width], s_pa[:, 0:width], s_pa[:, 0:width], OP.mult),
            inc=None)
        Vv = emit("vector", [], lambda e: e.tensor_tensor(
            s_pb[:, 0:width], s_pb[:, 0:width], s_S1[0:1, 0:width], OP.subtract),
            inc="V")
        emit("scalar", [("V", Vv)], lambda e: e.activation(
            s_pb[:, 0:width], s_pb[:, 0:width], AF.Ln, bias=s_eps[0:1, 0:1]),
            inc=None)
        Cv = emit("scalar", [], lambda e: e.activation(
            s_pb[:, 0:width], s_pb[:, 0:width], AF.Exp, scale=-0.5), inc="C")
        return emit("vector", [("C", Cv)], lambda e: e.scalar_tensor_tensor(
            s_pc[:, 0:width], s_pa[:, 0:width], -1.0, s_pb[:, 0:width],
            OP.mult, OP.mult), inc="V")

    def ln_bcast(V_ready, width):
        emit("tensor", [("V", V_ready), bank_cons[2]],
             lambda e: e.matmul(pb[2][:, 0:width], s_onesrow[:],
                                s_pb[:, 0:width], start=True, stop=True),
             inc="T")
        return emit("tensor", [bank_cons[3]],
                    lambda e: e.matmul(pb[3][:, 0:width], s_onesrow[:],
                                       s_pc[:, 0:width], start=True, stop=True),
                    inc="T")

    for p in range(NP):
        Lb = dma("sync", [mixA_free] if p > 0 else [], xsb[:], d_xb1[p], "L")
        xsb_ready = ("L", Lb)
        Lx = {}
        for i in range(ND):
            if p > 0:
                w = [xbuf_free[i] if isinstance(xbuf_free, list) else xbuf_free]
            else:
                w = []
            Lx[i] = ("L", dma("sync", w, xb(i, 128, 512),
                              d_x[p][i * 128:(i + 1) * 128, :], "L"))
        xb_ready = Lx[ND - 1]

        emit("vector", [attn2_free, xb_ready],
             lambda e: e.memset(ek_t(0, 0, 1), 0.0), inc=None)
        for j in range(ND):
            if j > 0:
                emit("vector", [], lambda e, j=j: e.memset(ek_t(j, 0, 1), 0.0),
                     inc=None)
            emit("vector", [], lambda e, j=j: e.memset(ekv_t(j, 0, 1), 0.0),
                 inc=None)
        emit("vector", [stats_free], lambda e: e.memset(s_statk[:], 0.0),
             inc=None)
        V_z = emit("vector", [], lambda e: e.memset(s_statv[:], 0.0),
                   inc="V")

        def mix_tile(i, dsth, dstl, tmvec, w0, eng="vector"):
            if eng == "vector":
                emit(eng, w0,
                     lambda e, i=i: e.tensor_tensor(
                         s_S1[:, 0:1], xb(i, 128, 1), xsb[:, i:i + 1],
                         OP.subtract), inc=None)
                emit(eng, [],
                     lambda e, i=i: e.tensor_tensor(
                         s_S1[:, 1:512], xb(i, 129, 511), xb(i, 128, 511),
                         OP.subtract), inc=None)
                emit(eng, [],
                     lambda e, i=i, tv=tmvec:
                     e.scalar_tensor_tensor(s_S2[:, 0:1], s_S1[:, 0:1],
                                            tv[:, i:i + 1], xsb[:, i:i + 1],
                                            OP.mult, OP.add), inc=None)
                emit(eng, [],
                     lambda e, i=i, tv=tmvec:
                     e.scalar_tensor_tensor(s_S2[:, 1:512], s_S1[:, 1:512],
                                            tv[:, i:i + 1], xb(i, 128, 511),
                                            OP.mult, OP.add), inc=None)
                T2, letter = s_S2, "V"
            else:
                T2, letter = mB(0), "P"
                emit(eng, w0,
                     lambda e, i=i, tv=tmvec: e.tensor_tensor(
                         s_G1[:], xb(i, 128, 512),
                         tv[:, i:i + 1].broadcast_to([128, 512]), OP.mult),
                     inc=None)
                emit(eng, [],
                     lambda e, i=i, T2=T2: e.tensor_tensor(
                         T2[:, 0:1], xsb[:, i:i + 1], sv_tmk1[:, i:i + 1],
                         OP.mult), inc=None)
                emit(eng, [],
                     lambda e, i=i, T2=T2: e.tensor_tensor(
                         T2[:, 1:512], xb(i, 128, 511),
                         sv_tmk1[:, i:i + 1].broadcast_to([128, 511]),
                         OP.mult), inc=None)
                emit(eng, [],
                     lambda e, i=i, T2=T2: e.tensor_tensor(
                         T2[:], T2[:], s_G1[:], OP.add), inc=None)
            emit(eng, [],
                 lambda e, i=i, d=dsth, T2=T2: e.tensor_copy(d(i), T2[:]),
                 inc=None)
            return (letter,
                    emit(eng, [],
                         lambda e, i=i, dh=dsth, dl=dstl, T2=T2:
                         e.tensor_tensor(dl(i), T2[:], dh(i), OP.subtract),
                         inc=letter))

        k_eng = "vector"
        V_xk = []
        for i in range(ND):
            w0 = [mixA_free, ("V", V_ones), xsb_ready] if i == 0 else []
            w0 = w0 + [Lx[i]]
            V_xk.append(mix_tile(i, mAh, mAl, sv_tmk, w0, eng=k_eng))

        def k_cons(j, bank, tv):
            ev = emit("scalar", [("T", tv), ("V", V_z)],
                      lambda e, j=j, b=bank:
                      e.activation(ek_t(j, 1, 512), pb[b][:], AF.Exp,
                                   accum_out=s_statk[:, j:j + 1]),
                      inc="C")
            return ("C", ev)
        C_ek = proj3(d_wk, mAh, mAl, k_cons, [V_xk[7]],
                     first_prep=[mixB_free],
                     lo_eng="gpsimd",
                     extra_second=[V_xk[-1]])
        T_k_done = n["T"]

        V_xv = []
        for i in range(ND):
            w0 = [mixB_free, xsb_ready] if i == 0 else []
            if i == 0 and p > 0:
                w0 = w0 + [V_xk[-1]]
            V_xv.append(mix_tile(i, mBh, mBl, sv_tmv, w0))

        def v_cons(j, bank, tv):
            ev = emit("vector", [("T", tv), C_ek[j]],
                      lambda e, j=j, b=bank:
                      e.scalar_tensor_tensor(ekv_t(j, 1, 512), pb[b][:], 1.0,
                                             ek_t(j, 1, 512), OP.mult, OP.mult,
                                             accum_out=s_statv[:, j:j + 1]),
                      inc="V")
            return ("V", ev)

        V_xr = []

        def mk_rmix(t):
            def th():
                w0 = [("T", T_k_done)] if t == 0 else []
                V_xr.append(mix_tile(t, mAh, mAl, sv_tmr, w0))
            return th
        v_drip = [[mk_rmix(idx // 2)] if idx % 2 == 0 else []
                  for idx in range(32)]
        V_ekv = proj3(d_wv, mBh, mBl, v_cons, [V_xv[-1]],
                      lo_eng="gpsimd", drip=v_drip)

        dma("gpsimd", [C_ek[ND - 1], V_ekv[ND - 1]],
            cc_in[p].ap()[:, 0:16], s_statk[:, 0:16], "G")
        Gv = dma("gpsimd", [],
                 cc_in[p].ap()[:, 16:32], s_statv[:, 0:16], "G")
        stats_free = ("G", Gv)
        Kv = emit("gpsimd", [("G", Gv)],
                  lambda e, ci=cc_in[p].ap(), co=cc_out[p].ap():
                  e.collective_compute(
                      "AllGather", OP.bypass,
                      replica_groups=[list(range(NCORES))],
                      ins=[ci], outs=[co]),
                  inc="K")
        g_last = 0
        for c in range(8):
            g_last = dma("gpsimd", [("K", Kv)] if c == 0 else [],
                         s_g[:, c * 32:(c + 1) * 32], cc_out[p].ap()[c], "G")

        V_wkv = [None] * ND

        def mk_scan(j):
            def th():
                emit("vector", [C_ek[j]],
                     lambda e, j=j: e.tensor_tensor_scan(
                         s_S1[:], s_onescol[:].broadcast_to([128, 512]), ek_t(j, 0, 512),
                         0.0, OP.mult, OP.add), inc=None)
                emit("vector", [],
                     lambda e, j=j: e.scalar_tensor_tensor(
                         ek_t(j, 1, 512), ek_t(j, 1, 512), sv_eu[:, j:j + 1],
                         s_S1[:], OP.mult, OP.add), inc=None)
                emit("vector", [],
                     lambda e, j=j: e.tensor_tensor_scan(
                         s_S2[:], s_onescol[:].broadcast_to([128, 512]), ekv_t(j, 0, 512),
                         0.0, OP.mult, OP.add), inc=None)
                emit("vector", [],
                     lambda e, j=j: e.scalar_tensor_tensor(
                         ekv_t(j, 1, 512), ekv_t(j, 1, 512), sv_eu[:, j:j + 1],
                         s_S2[:], OP.mult, OP.add), inc=None)
            return th

        def mk_offs():
            def th():
                if p == 0:
                    emit("vector", [("G", g_last), ("L", L_consts)],
                         lambda e: e.memset(s_offs[:], 0.0), inc=None)
                    emit("vector", [], lambda e: e.memset(s_tot[:], 0.0),
                         inc=None)
                    for c in range(8):
                        emit("vector", [],
                             lambda e, c=c: e.scalar_tensor_tensor(
                                 s_offs[:], s_g[:, c * 32:(c + 1) * 32],
                                 sv_mask[:, c:c + 1], s_offs[:],
                                 OP.mult, OP.add), inc=None)
                        emit("vector", [],
                             lambda e, c=c: e.scalar_tensor_tensor(
                                 s_tot[:], s_g[:, c * 32:(c + 1) * 32],
                                 1.0, s_tot[:], OP.mult, OP.add), inc=None)
                else:
                    emit("vector", [("G", g_last)],
                         lambda e: e.tensor_copy(s_offs[:], s_tot[:]),
                         inc=None)
                    for c in range(8):
                        emit("vector", [],
                             lambda e, c=c: e.scalar_tensor_tensor(
                                 s_offs[:], s_g[:, c * 32:(c + 1) * 32],
                                 sv_mask[:, c:c + 1], s_offs[:],
                                 OP.mult, OP.add), inc=None)
            return th

        def mk_wkv(j):
            def th():
                emit("vector", [],
                     lambda e, j=j: e.tensor_scalar(
                         ek_t(j, 1, 512), ek_t(j, 1, 512), s_offs[:, j:j + 1],
                         None, OP.add), inc=None)
                emit("vector", [],
                     lambda e, j=j: e.reciprocal(s_S1[:], ek_t(j, 1, 512)),
                     inc=None)
                V_wkv[j] = emit("vector", [],
                                lambda e, j=j: e.scalar_tensor_tensor(
                                    ekv_t(j, 1, 512), ekv_t(j, 1, 512),
                                    s_offs[:, 16 + j:17 + j], s_S1[:],
                                    OP.add, OP.mult), inc="V")
            return th

        T_sq2 = {}
        C_sq = {}

        def mk_stats(j):
            def th():
                C_sq[j] = emit("scalar", [("V", V_wkv[j])],
                               lambda e, j=j:
                               e.activation(ek_t(j, 1, 512), ekv_t(j, 1, 512),
                                            AF.Square),
                               inc="C")
                wS = [("C", C_sq[j])] + ([bank_cons[4]] if j == 0 else [])
                emit("tensor", wS,
                     lambda e, j=j: e.matmul(pb[4][0:1, :], s_onescol[:],
                                             ekv_t(j, 1, 512),
                                             start=(j == 0),
                                             stop=(j == ND - 1)),
                     inc="T" if j == ND - 1 else None)
                wS2 = [("C", C_sq[j])] + ([bank_cons[5]] if j == 0 else [])
                T_sq2[j] = emit("tensor", wS2,
                                lambda e, j=j:
                                e.matmul(pb[5][32:33, :], s_onescol[:],
                                         ek_t(j, 1, 512),
                                         start=(j == 0), stop=(j == ND - 1),
                                         tile_position=(0, 32)),
                                inc="T")
            return th

        r_drip = [[mk_scan(idx)] for idx in range(16)]
        r_drip.append([mk_offs()])
        r_drip += [[mk_wkv(idx - 17)] for idx in range(17, 32)]
        for j in range(11):
            r_drip[20 + j].append(mk_stats(j))

        def r_cons(j, bank, tv):
            ev = emit("scalar", [("T", tv)],
                      lambda e, j=j, b=bank:
                      e.activation(mB(j), pb[b][:], AF.Sigmoid),
                      inc="C")
            return ("C", ev)
        C_r = proj3(d_wr, mAh, mAl, r_cons, [V_xr[-1]],
                    lo_eng="vector", drip=r_drip)
        mk_wkv(ND - 1)()

        for j in range(11, ND):
            mk_stats(j)()
        Vp1 = ln_params(n["T"], 512)

        V_rwkv = [None] * ND
        P_hist = {}

        def d_of(j):
            return mixA[:, (j % 4) * 512:(j % 4) * 512 + 512]

        for j in range(ND):
            bkA = 2 + 2 * (j % 2)
            bkB = 3 + 2 * (j % 2)
            wA = [("V", Vp1), bank_cons[bkA]]
            TbA = emit("tensor", wA,
                       lambda e, j=j, bkA=bkA:
                       e.matmul(pb[bkA][:], sv_lngr[:, j * 128:(j + 1) * 128],
                                s_pb[:, 0:512], start=True, stop=True),
                       inc=None)
            TbB = emit("tensor", [bank_cons[bkB]],
                       lambda e, j=j, bkB=bkB:
                       e.matmul(pb[bkB][:], sv_lngr[:, j * 128:(j + 1) * 128],
                                s_pc[:, 0:512], start=True, stop=True),
                       inc="T")
            d = d_of(j)
            w = [("T", TbB)]
            if j == 0:
                w.append(("C", C_r[ND - 1][1]))
            if j >= 4:
                w.append(("P", P_hist[j - 4]))
            emit("vector", w,
                 lambda e, j=j, d=d, bkA=bkA:
                 e.tensor_tensor(d[:], ekv_t(j, 1, 512), pb[bkA][:], OP.mult),
                 inc=None)
            emit("vector", [],
                 lambda e, d=d, bkB=bkB: e.tensor_tensor(d[:], d[:], pb[bkB][:],
                                                         OP.add), inc=None)
            Vm = emit("vector", [],
                      lambda e, j=j, d=d: e.scalar_tensor_tensor(
                          d[:], d[:], sv_lnb[:, j:j + 1], mB(j),
                          OP.add, OP.mult), inc="V")
            bank_cons[bkA] = ("V", Vm)
            bank_cons[bkB] = ("V", Vm)
            Ch = emit("scalar", [("V", Vm)],
                      lambda e, j=j, d=d: e.copy(rw_h(j), d[:]), inc="C")
            P_hist[j] = emit("gpsimd", [("C", Ch)],
                             lambda e, j=j, d=d:
                             e.tensor_tensor(rw_l(j), d[:], rw_h(j),
                                             OP.subtract), inc="P")
            V_rwkv[j] = ("P", P_hist[j])
        bank_cons[2] = V_rwkv[-1]
        bank_cons[3] = V_rwkv[-1]

        V_x2pre = []

        def wo_cons(j2, bank, tv):
            Vv = emit("vector", [("T", tv)],
                      lambda e, j2=j2, b=bank:
                      e.tensor_tensor(xb(j2, 128, 512), pb[b][:],
                                      xb(j2, 128, 512), OP.add), inc="V")
            V_x2pre.append(Vv)
            return ("V", Vv)
        proj3(d_wo, rw_h, rw_l, wo_cons, [V_rwkv[7]],
              extra_second=[V_rwkv[-1]])

        def prep_w(dram_ap, cols, f, extra):
            Lv = dma("sync", wb_cons[f % 2], wb[f % 2][:, 0:cols], dram_ap, "L")
            Ch = emit("scalar", [("L", Lv), w16_cons[f % 2]] + extra,
                      lambda e, f=f, cols=cols:
                      e.copy(w16[f % 2][:, 0:cols], wb[f % 2][:, 0:cols]),
                      inc="C")
            Vl = emit("vector", [("C", Ch), wbl_cons[f % 2]],
                      lambda e, f=f, cols=cols:
                      e.tensor_tensor(wbl[f % 2][:, 0:cols],
                                      wb[f % 2][:, 0:cols],
                                      w16[f % 2][:, 0:cols], OP.subtract),
                      inc="V")
            wb_cons[f % 2] = [("C", Ch), ("V", Vl)]
            return (Ch, Vl)

        Lw0 = dma("sync", wb_cons[0], wb[0][:, 0:2048], d_w0[0], "L")

        T_sq2 = {}
        for j in range(ND):
            sq = s_S2 if j % 2 == 0 else s_S1
            wq = [("V", V_x2pre[j])]
            if j >= 2:
                wq.append(("T", T_sq2[j - 2]))
            Cs = emit("scalar", wq,
                      lambda e, j=j, sq=sq: e.activation(sq[:], xb(j, 128, 512),
                                                         AF.Square), inc="C")
            wS = [("C", Cs)] + ([bank_cons[4]] if j == 0 else [])
            emit("tensor", wS,
                 lambda e, j=j: e.matmul(pb[4][0:1, :], s_onescol[:],
                                         xb(j, 128, 512),
                                         start=(j == 0), stop=(j == ND - 1)),
                 inc="T" if j == ND - 1 else None)
            wS2 = [("C", Cs)] + ([bank_cons[5]] if j == 0 else [])
            T_sq2[j] = emit("tensor", wS2,
                            lambda e, j=j, sq=sq: e.matmul(pb[5][32:33, :],
                                                           s_onescol[:],
                                                           sq[:], start=(j == 0),
                                                           stop=(j == ND - 1),
                                                           tile_position=(0, 32)),
                            inc="T")
        Vp2 = ln_params(n["T"], 512)
        ChA = emit("scalar", [("L", Lw0), w16_cons[0], V_rwkv[-1], mixB_free],
                   lambda e: e.copy(w16[0][:, 0:1024], wb[0][:, 0:1024]),
                   inc="C")
        ChB = emit("scalar", [],
                   lambda e: e.copy(w16[0][:, 1024:2048], wb[0][:, 1024:2048]),
                   inc="C")
        VlW = emit("vector", [("C", ChB), wbl_cons[0]],
                   lambda e: e.tensor_tensor(wbl[0][:], wb[0][:], w16[0][:],
                                             OP.subtract), inc="V")
        wb_cons[0] = [("C", ChB), ("V", VlW)]
        Tb2 = ln_bcast(Vp2, 512)

        C_xq = {}
        V_xq = {}

        def emit_norm2(j):
            w = [("T", Tb2)] if j == 0 else []
            emit("vector", w,
                 lambda e, j=j: e.tensor_tensor(xb(j, 128, 512), xb(j, 128, 512),
                                                pb[2][:], OP.mult), inc=None)
            Vn = emit("vector", [],
                      lambda e, j=j: e.tensor_tensor(xb(j, 128, 512),
                                                     xb(j, 128, 512),
                                                     pb[3][:], OP.add), inc="V")
            C_xq[j] = emit("scalar", [("V", Vn)],
                           lambda e, j=j: e.copy(xq16_t(j), xb(j, 128, 512)),
                           inc="C")

        def emit_xqb(j):
            V_xq[j] = emit("gpsimd", [("C", C_xq[j])],
                           lambda e, j=j: e.tensor_copy(xqb_t(j), xq16_t(j)),
                           inc="P")

        emit_norm2(0)
        for j in range(1, ND):
            emit_norm2(j)
            emit_xqb(j - 1)
        emit_xqb(ND - 1)
        bank_cons[2] = ("V", n["V"])
        bank_cons[3] = ("V", n["V"])
        T_attn_done = n["T"]

        def prep_hg(f):
            return emit("vector", [("C", C_gelu[f]), hgb_cons[f % 2]],
                        lambda e, f=f: e.tensor_copy(hgb[f % 2][:, 0:512],
                                                     hg16_t(f)),
                        inc="V")

        C_gelu = []
        prepped = {0: (ChA, VlW)}
        for f in range(NF):
            Ch, Vl = prepped[f]
            bank = f % 2
            waits = [("C", Ch), bank_cons[bank]]
            if f == 0:
                waits += [("T", T_attn_done)]
            for i in range(ND):
                w_i = list(waits) if i == 0 else []
                if f == 0:
                    w_i.append(("C", C_xq[i]))
                    if i == 8:
                        w_i.append(("C", ChB))
                emit("tensor", w_i,
                     lambda e, b=bank, f=f, i=i:
                     e.matmul(pb[b][:], w16[f % 2][:, i * 128:(i + 1) * 128],
                              xq16_t(i), start=(i == 0), stop=False),
                     inc=None)
            for i in range(ND):
                w_i = [("V", Vl)] if i == 0 else []
                if f == 0:
                    w_i.append(("P", V_xq[i]))
                lastv = emit("tensor", w_i,
                             lambda e, b=bank, f=f, i=i:
                             e.matmul(pb[b][:], wbl[f % 2][:, i * 128:(i + 1) * 128],
                                      xqb_t(i), start=False, stop=(i == ND - 1)),
                             inc="T" if i == ND - 1 else None)
            w16_cons[f % 2] = ("T", lastv)
            wbl_cons[f % 2] = ("T", lastv)
            if f + 1 < NF:
                prepped[f + 1] = prep_w(d_w0[f + 1], 2048, f + 1,
                                        [V_rwkv[-1]] if f == 0 else [])
            Cg = emit("scalar", [("T", lastv)],
                      lambda e, f=f, b=bank:
                      e.activation(hg16_t(f), pb[b][:], AF.Gelu,
                                   bias=sv_b0[:, f:f + 1]), inc="C")
            C_gelu.append(Cg)
            bank_cons[bank] = ("C", Cg)
            if f == NF - 2:
                w1_early = prep_w(d_w1[0][0], 1024, 0, [])
        T_ffn1 = n["T"]

        V_x3 = {}
        for eh in range(2):
            prepped = {0: (w1_early if eh == 0
                           else prep_w(d_w1[eh][0], 1024, 0, []))}
            hgprep = {0: prep_hg(0)}
            for f in range(NF):
                Ch, Vl = prepped[f]
                Vc = hgprep[f]
                waits = [("C", Ch), ("C", C_gelu[f])]
                for e8 in range(8):
                    w = (waits if e8 == 0 else [])                         + ([bank_cons[e8]] if f == 0 else [])
                    emit("tensor", w,
                         lambda e, f=f, e8=e8:
                         e.matmul(pb[e8][:], w16[f % 2][:, e8 * 128:(e8 + 1) * 128],
                                  hg16_t(f), start=(f == 0), stop=False),
                         inc=None)
                vw = [("V", Vl), ("V", Vc)]
                for e8 in range(8):
                    lastv = emit("tensor", vw if e8 == 0 else [],
                                 lambda e, f=f, e8=e8:
                                 e.matmul(pb[e8][:],
                                          wbl[f % 2][:, e8 * 128:(e8 + 1) * 128],
                                          hgb[f % 2][:, 0:512], start=False,
                                          stop=(f == NF - 1)),
                                 inc="T" if e8 == 7 else None)
                w16_cons[f % 2] = ("T", lastv)
                wbl_cons[f % 2] = ("T", lastv)
                hgb_cons[f % 2] = ("T", lastv)
                if f + 1 < NF:
                    prepped[f + 1] = prep_w(d_w1[eh][f + 1], 1024, f + 1, [])
                    hgprep[f + 1] = prep_hg(f + 1)
            T_close = lastv
            for e8 in range(8):
                je = eh * 8 + e8
                w = [("T", T_close)] if e8 == 0 else []
                emit("vector", w,
                     lambda e, e8=e8, je=je: e.tensor_scalar(
                         s_h16[e8 % 2][:, 0:512], pb[e8][:],
                         sv_b1[:, je:je + 1], None, OP.add), inc=None)
                Vv = emit("vector", [],
                          lambda e, e8=e8, je=je: e.tensor_tensor(
                              xb(je, 128, 512), s_h16[e8 % 2][:, 0:512],
                              xb(je, 128, 512), OP.add), inc="V")
                bank_cons[e8] = ("V", Vv)
                V_x3[je] = Vv
        T_ffn2 = n["T"]

        T_sq2 = {}
        for j in range(ND):
            sq = s_S2 if j % 2 == 0 else s_S1
            wq = [("V", V_x3[j])]
            if j >= 2:
                wq.append(("T", T_sq2[j - 2]))
            Cs = emit("scalar", wq,
                      lambda e, j=j, sq=sq:
                      e.activation(sq[:], xb(j, 128, 512), AF.Square),
                      inc="C")
            wS = [("C", Cs)] + ([bank_cons[4]] if j == 0 else [])
            emit("tensor", wS,
                 lambda e, j=j: e.matmul(pb[4][0:1, :], s_onescol[:],
                                         xb(j, 128, 512),
                                         start=(j == 0), stop=(j == ND - 1)),
                 inc="T" if j == ND - 1 else None)
            wS2 = [("C", Cs)] + ([bank_cons[5]] if j == 0 else [])
            T_sq2[j] = emit("tensor", wS2,
                            lambda e, j=j, sq=sq: e.matmul(pb[5][32:33, :],
                                                           s_onescol[:],
                                                           sq[:], start=(j == 0),
                                                           stop=(j == ND - 1),
                                                           tile_position=(0, 32)),
                            inc="T")
        Vp3 = ln_params(n["T"], 512)
        Tb3 = ln_bcast(Vp3, 512)
        V_o = []
        for j in range(ND):
            w = [("T", Tb3)] if j == 0 else []
            emit("vector", w,
                 lambda e, j=j: e.tensor_tensor(
                     out_t(j), xb(j, 128, 512), pb[2][:], OP.mult),
                 inc=None)
            Vn = emit("vector", [],
                      lambda e, j=j: e.tensor_tensor(
                          out_t(j), out_t(j), pb[3][:], OP.add), inc="V")
            V_o.append(Vn)
            Gs = dma("gpsimd", [("V", Vn)],
                     d_out[j * 128:(j + 1) * 128, p * 512:(p + 1) * 512],
                     out_t(j), "G")
        bank_cons[2] = ("V", V_o[-1])
        bank_cons[3] = ("V", V_o[-1])
        xbuf_free_now = [("V", v) for v in V_o]
        mixB_free = ("T", T_ffn2)

        mixA_free = ("T", T_ffn1)
        attn2_free = ("G", n["G"])
        xbuf_free = xbuf_free_now

    emit("gpsimd", [("G", n["G"]), ("T", n["T"]), ("V", n["V"]),
                    ("C", n["C"]), ("L", n["L"]), ("P", n["P"])],
         lambda e: e.nop(), inc=None)

    with nc.Block() as block:
        @block.sync
        def _(e):
            for f in q["sync"]:
                f(e)

        @block.gpsimd
        def _(e):
            for f in q["gpsimd"]:
                f(e)

        @block.tensor
        def _(e):
            for f in q["tensor"]:
                f(e)

        @block.vector
        def _(e):
            for f in q["vector"]:
                f(e)

        @block.scalar
        def _(e):
            for f in q["scalar"]:
                f(e)

    return nc


_PROG_CACHE = {}


def _get_program():
    if "nc" not in _PROG_CACHE:
        _PROG_CACHE["nc"] = build_program()
    return _PROG_CACHE["nc"]


def _v128(v):
    v = np.asarray(v, dtype=np.float32).reshape(-1)
    return np.ascontiguousarray(v.reshape(-1, 128).T)


def _tile_wt(w):
    wt = np.asarray(w, dtype=np.float32).T
    din, dout = wt.shape
    ni, no = din // 128, dout // 128
    return np.ascontiguousarray(
        wt.reshape(ni, 128, no, 128).transpose(2, 1, 0, 3).reshape(no, 128, din))


def kernel(L0_attn_x, L0_attn_Wk, L0_attn_Wv, L0_attn_Wr, L0_attn_Wo,
           L0_attn_tm_k, L0_attn_tm_v, L0_attn_tm_r, L0_attn_time_decay,
           L0_attn_time_first, L0_attn_ln_g, L0_attn_ln_b,
           L0_ffn_W0, L0_ffn_bias0, L0_ffn_W1, L0_ffn_bias1):
    x2d = np.ascontiguousarray(np.asarray(L0_attn_x, dtype=np.float32)
                               .reshape(8192, 2048))

    w1t = np.asarray(L0_ffn_W1, dtype=np.float32).T
    w1_t = np.ascontiguousarray(
        w1t.reshape(64, 128, 2, 8, 128).transpose(2, 0, 1, 3, 4)
        .reshape(2, 64, 128, 1024))

    common = {
        "wk": _tile_wt(L0_attn_Wk), "wv": _tile_wt(L0_attn_Wv),
        "wr": _tile_wt(L0_attn_Wr), "wo": _tile_wt(L0_attn_Wo),
        "w0": _tile_wt(L0_ffn_W0), "w1": w1_t,
        "tmk": _v128(L0_attn_tm_k), "tmv": _v128(L0_attn_tm_v),
        "tmr": _v128(L0_attn_tm_r), "tf": _v128(L0_attn_time_first),
        "lng": _v128(L0_attn_ln_g), "lnb": _v128(L0_attn_ln_b),
        "lngr": np.ascontiguousarray(np.asarray(L0_attn_ln_g, dtype=np.float32)
                                     .reshape(1, 2048)),
        "b1": _v128(L0_ffn_bias1), "b0": _v128(L0_ffn_bias0),
        "ident": np.eye(128, dtype=np.float32),
    }

    in_maps = []
    for c in range(NCORES):
        m = dict(common)
        for p in range(NP):
            s = p * 4096 + c * 512
            m[f"xin{p}"] = np.ascontiguousarray(x2d[s: s + 512].T)
            if s == 0:
                xb1 = np.zeros((128, 16), np.float32)
            else:
                xb1 = np.ascontiguousarray(x2d[s - 1].reshape(16, 128).T)
            m[f"xb1{p}"] = xb1
        m["mask"] = np.asarray(
            [[1.0 if cc < c else 0.0 for cc in range(8)]], dtype=np.float32)
        in_maps.append(m)

    nc = _get_program()
    res = run_bass_kernel_spmd(nc, in_maps, core_ids=list(range(NCORES)))

    full = np.empty((8192, 2048), dtype=np.float32)
    for c in range(NCORES):
        o = np.asarray(res.results[c]["out"])
        for p in range(NP):
            s = p * 4096 + c * 512
            full[s: s + 512] = o[:, p * 512:(p + 1) * 512].T
    return full.reshape(1, 8192, 2048)


# version 35
# speedup vs baseline: 1.0058x; 1.0058x over previous
import numpy as np
import concourse.bass as bass
import concourse.mybir as mybir
from concourse.bass_utils import run_bass_kernel_spmd

F32 = mybir.dt.float32
F16 = mybir.dt.float16
BF16 = mybir.dt.bfloat16
AF = mybir.ActivationFunctionType
OP = mybir.AluOpType

NCORES = 8
D = 2048
ND = 16
T = 512
NP = 2
FF = 8192
NF = 64
TC = 256
EPS = 1e-5
INVD = 1.0 / 2048.0


def build_program():
    nc = bass.Bass("TRN2", target_bir_lowering=False, debug=False,
                   detect_race_conditions=False)

    d_x = [nc.dram_tensor(f"xin{p}", [2048, 512], F32, kind="ExternalInput").ap()
           for p in range(NP)]
    d_xb1 = [nc.dram_tensor(f"xb1{p}", [128, 16], F32, kind="ExternalInput").ap()
             for p in range(NP)]
    d_wk = nc.dram_tensor("wk", [16, 128, 2048], F32, kind="ExternalInput").ap()
    d_wv = nc.dram_tensor("wv", [16, 128, 2048], F32, kind="ExternalInput").ap()
    d_wr = nc.dram_tensor("wr", [16, 128, 2048], F32, kind="ExternalInput").ap()
    d_wo = nc.dram_tensor("wo", [16, 128, 2048], F32, kind="ExternalInput").ap()
    d_w0 = nc.dram_tensor("w0", [64, 128, 2048], F32, kind="ExternalInput").ap()
    d_w1 = nc.dram_tensor("w1", [2, 64, 128, 1024], F32, kind="ExternalInput").ap()
    d_tmk = nc.dram_tensor("tmk", [128, 16], F32, kind="ExternalInput").ap()
    d_tmv = nc.dram_tensor("tmv", [128, 16], F32, kind="ExternalInput").ap()
    d_tmr = nc.dram_tensor("tmr", [128, 16], F32, kind="ExternalInput").ap()
    d_tf = nc.dram_tensor("tf", [128, 16], F32, kind="ExternalInput").ap()
    d_lng = nc.dram_tensor("lng", [128, 16], F32, kind="ExternalInput").ap()
    d_lngr = nc.dram_tensor("lngr", [1, 2048], F32, kind="ExternalInput").ap()
    d_lnb = nc.dram_tensor("lnb", [128, 16], F32, kind="ExternalInput").ap()
    d_b1 = nc.dram_tensor("b1", [128, 16], F32, kind="ExternalInput").ap()
    d_b0 = nc.dram_tensor("b0", [128, 64], F32, kind="ExternalInput").ap()
    d_mask = nc.dram_tensor("mask", [1, 8], F32, kind="ExternalInput").ap()
    d_id = nc.dram_tensor("ident", [128, 128], F32, kind="ExternalInput").ap()
    d_out = nc.dram_tensor("out", [2048, 1024], F32, kind="ExternalOutput").ap()
    cc_in = [nc.dram_tensor(f"cc_in{p}", [128, 32], F32) for p in range(NP)]
    cc_out = [nc.dram_tensor(f"cc_out{p}", [8, 128, 32], F32) for p in range(NP)]

    A = nc.alloc_sbuf_tensor
    sv_tmk = A("sv_tmk", [128, 16], F32)
    sv_tmk1 = A("sv_tmk1", [128, 16], F32)
    sv_tmv = A("sv_tmv", [128, 16], F32)
    sv_tmr = A("sv_tmr", [128, 16], F32)
    sv_tf = A("sv_tf", [128, 16], F32)
    sv_eu = A("sv_eu", [128, 16], F32)
    sv_lng = A("sv_lng", [128, 16], F32)
    sv_lngr = A("sv_lngr", [1, 2048], F32)
    sv_lnb = A("sv_lnb", [128, 16], F32)
    sv_b1 = A("sv_b1", [128, 16], F32)
    sv_b0 = A("sv_b0", [128, 64], F32)
    sv_mask = A("sv_mask", [128, 8], F32)
    s_id = A("s_id", [128, 128], F32)
    s_onescol = A("s_onescol", [128, 1], F32)
    s_onesrow = A("s_onesrow", [1, 128], F32)
    s_statk = A("s_statk", [128, 32], F32)
    s_statv = A("s_statv", [128, 32], F32)
    s_g = A("s_g", [128, 256], F32)
    s_offs = A("s_offs", [128, 32], F32)
    s_tot = A("s_tot", [128, 32], F32)
    s_eps = A("s_eps", [1, 1], F32)
    s_pa = A("s_pa", [1, 512], F32)
    s_pb = A("s_pb", [1, 512], F32)
    s_pc = A("s_pc", [1, 512], F32)
    s_tm1 = {}
    s_G1 = A("s_G1", [128, 512], F32)
    s_S1 = A("s_S1", [128, 512], F32)
    s_S2 = A("s_S2", [128, 512], F32)

    xbuf = A("xbuf", [128, ND * 512], F32)
    xsb = A("xsb", [128, 16], F32)
    mixA = A("mixA", [128, ND * 512], F32)
    mixB = A("mixB", [128, ND * 512], F32)
    attn2 = A("attn2", [128, 2 * ND * 513], F32)
    wb = [A(f"wb{k}", [128, 2048], F32) for k in range(2)]
    aw_h = [A(f"aw_h{k}", [128, 1024], BF16) for k in range(2)]
    aw_l = [A(f"aw_l{k}", [128, 1024], BF16) for k in range(2)]

    def xb(i, c0, w):
        return xbuf[:, i * 512 + c0 - 128: i * 512 + c0 - 128 + w]

    def mA(i, c0=0, w=512):
        return mixA[:, i * 512 + c0: i * 512 + c0 + w]

    def mB(i, c0=0, w=512):
        return mixB[:, i * 512 + c0: i * 512 + c0 + w]

    def ek_t(j, c0, w):
        return attn2[:, j * 513 + c0: j * 513 + c0 + w]

    def ekv_t(j, c0, w):
        return attn2[:, ND * 513 + j * 513 + c0: ND * 513 + j * 513 + c0 + w]

    def hg_t(f):
        return attn2[:, f * TC: (f + 1) * TC]

    def stg(k):
        return mixB[:, k * 2048: (k + 1) * 2048]

    xstage = [mixA[:, k * 2048: (k + 1) * 2048] for k in range(4)]

    mixA_f16 = mixA[:].bitcast(F16)
    mixA_bf = mixA[:].bitcast(BF16)

    def xq16_t(j):
        return mixA_f16[:, j * 512:(j + 1) * 512]

    def xqb_t(j):
        return mixA_bf[:, 8192 + j * 512: 8192 + (j + 1) * 512]

    def mAh(i):
        base = 4096 * (i % 4) + 1024 * (i // 4)
        return mixA_bf[:, base: base + 512]

    def mAl(i):
        base = 4096 * (i % 4) + 1024 * (i // 4)
        return mixA_bf[:, base + 512: base + 1024]

    def mBh(i):
        return mixB_bf[:, i * 512:(i + 1) * 512]

    def mBl(i):
        return mixB_bf[:, 8192 + i * 512: 8192 + (i + 1) * 512]

    attn2_bf = attn2[:].bitcast(BF16)

    def rw_h(j):
        return attn2_bf[:, 1026 * j: 1026 * j + 512]

    def rw_l(j):
        return attn2_bf[:, 1026 * j + 512: 1026 * j + 1024]

    attn2_f16 = attn2[:].bitcast(F16)

    def hg16_t(f):
        return attn2_f16[:, f * 512:(f + 1) * 512]

    def out_t(j):
        return attn2[:, j * 512:(j + 1) * 512]

    mixB_f16 = mixB[:].bitcast(F16)
    mixB_bf = mixB[:].bitcast(BF16)
    w16 = [mixB_f16[:, k * 2048:(k + 1) * 2048] for k in range(2)]
    w16b = [mixB_bf[:, k * 2048:(k + 1) * 2048] for k in range(2)]
    wbl = [mixB_bf[:, 4096 + k * 2048: 4096 + (k + 1) * 2048] for k in range(2)]
    hgb = [mixB_bf[:, 8192 + k * 512: 8192 + (k + 1) * 512] for k in range(2)]
    s_h16 = [mixB_f16[:, 9216 + k * 512: 9216 + (k + 1) * 512] for k in range(2)]

    pb = [nc.alloc_psum_tensor(f"pb{k}", [128, 512], F32) for k in range(8)]

    sem = {nm: nc.alloc_semaphore(nm) for nm in ("L", "G", "T", "V", "C", "K", "P")}
    n = {nm: 0 for nm in sem}
    q = {e: [] for e in ("sync", "gpsimd", "tensor", "vector", "scalar")}

    def emit(eng, waits, f, inc=None, inc_n=1):
        for s, v in waits:
            assert v <= n[s], f"wait {s}>={v} but only {n[s]} emitted"
        ws = [(sem[s], v) for s, v in waits if v > 0]
        if inc is not None:
            n[inc] += inc_n

        def g(e, ws=ws, f=f, h=(sem[inc] if inc else None), hn=inc_n):
            for s, v in ws:
                e.wait_ge(s, v)
            inst = f(e)
            if h is not None:
                inst.then_inc(h, hn)
        q[eng].append(g)
        return n[inc] if inc is not None else None

    def dma(eng_q, waits, out_ap, in_ap, inc):
        return emit(eng_q, waits,
                    lambda e, o=out_ap, i=in_ap: e.dma_start(out=o, in_=i),
                    inc=inc, inc_n=16)

    for dst, src in [(sv_tmk, d_tmk), (sv_tmv, d_tmv), (sv_tmr, d_tmr),
                     (sv_tf, d_tf), (sv_lng, d_lng), (sv_lnb, d_lnb),
                     (sv_b1, d_b1), (sv_b0, d_b0), (s_id, d_id),
                     (sv_lngr, d_lngr)]:
        dma("sync", [], dst[:], src, "L")
    dma("sync", [], sv_mask[:], d_mask.broadcast_to([128, 8]), "L")
    L_consts = n["L"]
    emit("vector", [], lambda e: e.memset(s_onescol[:], 1.0), inc=None)
    emit("vector", [], lambda e: e.memset(s_eps[:], float(EPS)), inc=None)
    emit("vector", [("L", L_consts)], lambda e: e.tensor_scalar(
        sv_tmk1[:], sv_tmk[:], -1.0, None, OP.mult), inc=None)
    emit("vector", [], lambda e: e.tensor_scalar(
        sv_tmk1[:], sv_tmk1[:], 1.0, None, OP.add), inc=None)
    V_ones = emit("vector", [], lambda e: e.memset(s_onesrow[:], 1.0), inc="V")
    C_eu = emit("scalar", [("L", L_consts)],
                lambda e: e.activation(sv_eu[:], sv_tf[:], AF.Exp), inc="C")

    wb_cons = [[("T", 0)], [("T", 0)]]
    awh_cons = [("T", 0), ("T", 0)]
    awl_cons = [("T", 0), ("T", 0)]
    w16_cons = [("T", 0), ("T", 0)]
    wbl_cons = [("T", 0), ("T", 0)]
    hgb_cons = [("T", 0), ("T", 0)]
    bank_cons = [("T", 0)] * 8
    mixA_free = ("T", 0)
    mixB_free = ("T", 0)
    attn2_free = ("T", 0)
    pre_next = None
    stats_free = ("G", 0)
    h16_free = ("V", 0)
    V_tot_p0 = 0
    stg_store = {b: ("G", 0) for b in range(4)}

    def prep_wa(dram_w, j, sc, k, extra, lo_eng="gpsimd"):
        lo_sem = "P" if lo_eng == "gpsimd" else "V"
        Lv = dma("sync", wb_cons[k], wb[k][:, 0:1024],
                 dram_w[j][:, sc * 1024:(sc + 1) * 1024], "L")
        Ch = emit("scalar", [("L", Lv), awh_cons[k]] + extra,
                  lambda e, k=k: e.copy(aw_h[k][:], wb[k][:, 0:1024]),
                  inc="C")
        Vl = emit(lo_eng, [("C", Ch), awl_cons[k]],
                  lambda e, k=k: e.tensor_tensor(aw_l[k][:], wb[k][:, 0:1024],
                                                 aw_h[k][:], OP.subtract),
                  inc=lo_sem)
        wb_cons[k] = [("C", Ch), (lo_sem, Vl)]
        return (Ch, (lo_sem, Vl))

    def proj3(dram_w, rhsh_fn, rhsl_fn, consumer, extra_first, first_prep=(),
              lo_eng="gpsimd", drip=None, extra_second=()):
        out = []
        seq = [(j, sc) for j in range(ND) for sc in (0, 1)]
        prepped = {0: prep_wa(dram_w, 0, 0, 0, list(first_prep), lo_eng)}
        for idx, (j, sc) in enumerate(seq):
            Ch, Vl = prepped[idx]
            k = idx % 2
            bank = j % 2
            w0 = [("C", Ch)] + ([bank_cons[bank]] if sc == 0 else []) \
                + (extra_first if idx == 0 else []) \
                + (list(extra_second) if idx == 1 else [])
            if drip is not None and idx < len(drip):
                for th in drip[idx]:
                    th()
            i0 = sc * 8
            for q in range(8):
                emit("tensor", w0 if q == 0 else [],
                     lambda e, b=bank, k=k, q=q, i=i0 + q, sc=sc:
                     e.matmul(pb[b][:], aw_h[k][:, q * 128:(q + 1) * 128],
                              rhsh_fn(i), start=(sc == 0 and q == 0),
                              stop=False),
                     inc=None)
                emit("tensor", [],
                     lambda e, b=bank, k=k, q=q, i=i0 + q:
                     e.matmul(pb[b][:], aw_h[k][:, q * 128:(q + 1) * 128],
                              rhsl_fn(i), start=False, stop=False),
                     inc=None)
            for q in range(8):
                lastv = emit("tensor", [Vl] if q == 0 else [],
                             lambda e, b=bank, k=k, q=q, i=i0 + q, sc=sc:
                             e.matmul(pb[b][:], aw_l[k][:, q * 128:(q + 1) * 128],
                                      rhsh_fn(i), start=False,
                                      stop=(sc == 1 and q == 7)),
                             inc="T" if q == 7 else None)
            awh_cons[k] = ("T", lastv)
            awl_cons[k] = ("T", lastv)
            if idx + 1 < len(seq):
                j2, sc2 = seq[idx + 1]
                prepped[idx + 1] = prep_wa(dram_w, j2, sc2, (idx + 1) % 2, [],
                                           lo_eng)
            if sc == 1:
                ev = consumer(j, bank, lastv)
                bank_cons[bank] = ev
                out.append(ev)
        return out

    def ln_params(T_ready, width):
        emit("vector", [("T", T_ready)], lambda e: e.tensor_scalar(
            s_pa[:, 0:width], pb[4][0:1, 0:width], INVD, None, OP.mult),
            inc=None)
        Vc = emit("vector", [], lambda e: e.tensor_scalar(
            s_pb[:, 0:width], pb[5][32:33, 0:width], INVD, None, OP.mult),
            inc="V")
        bank_cons[4] = ("V", Vc)
        bank_cons[5] = ("V", Vc)
        emit("vector", [], lambda e: e.tensor_tensor(
            s_S1[0:1, 0:width], s_pa[:, 0:width], s_pa[:, 0:width], OP.mult),
            inc=None)
        Vv = emit("vector", [], lambda e: e.tensor_tensor(
            s_pb[:, 0:width], s_pb[:, 0:width], s_S1[0:1, 0:width], OP.subtract),
            inc="V")
        emit("scalar", [("V", Vv)], lambda e: e.activation(
            s_pb[:, 0:width], s_pb[:, 0:width], AF.Ln, bias=s_eps[0:1, 0:1]),
            inc=None)
        Cv = emit("scalar", [], lambda e: e.activation(
            s_pb[:, 0:width], s_pb[:, 0:width], AF.Exp, scale=-0.5), inc="C")
        return emit("vector", [("C", Cv)], lambda e: e.scalar_tensor_tensor(
            s_pc[:, 0:width], s_pa[:, 0:width], -1.0, s_pb[:, 0:width],
            OP.mult, OP.mult), inc="V")

    def ln_bcast(V_ready, width):
        emit("tensor", [("V", V_ready), bank_cons[2]],
             lambda e: e.matmul(pb[2][:, 0:width], s_onesrow[:],
                                s_pb[:, 0:width], start=True, stop=True),
             inc="T")
        return emit("tensor", [bank_cons[3]],
                    lambda e: e.matmul(pb[3][:, 0:width], s_onesrow[:],
                                       s_pc[:, 0:width], start=True, stop=True),
                    inc="T")

    for p in range(NP):
        if pre_next is not None:
            xsb_ready = pre_next["xsb"]
            Lx = pre_next["Lx"]
            V_xk_pre = pre_next["V_xk"]
        else:
            Lb = dma("sync", [mixA_free] if p > 0 else [], xsb[:], d_xb1[p],
                     "L")
            xsb_ready = ("L", Lb)
            Lx = {}
            for i in range(ND):
                Lx[i] = ("L", dma("sync", [], xb(i, 128, 512),
                                  d_x[p][i * 128:(i + 1) * 128, :], "L"))
            V_xk_pre = None
        xb_ready = Lx[ND - 1]

        emit("vector", [attn2_free, xb_ready],
             lambda e: e.memset(ek_t(0, 0, 1), 0.0), inc=None)
        for j in range(ND):
            if j > 0:
                emit("vector", [], lambda e, j=j: e.memset(ek_t(j, 0, 1), 0.0),
                     inc=None)
            emit("vector", [], lambda e, j=j: e.memset(ekv_t(j, 0, 1), 0.0),
                 inc=None)
        emit("vector", [stats_free], lambda e: e.memset(s_statk[:], 0.0),
             inc=None)
        V_z = emit("vector", [], lambda e: e.memset(s_statv[:], 0.0),
                   inc="V")

        def mix_tile(i, dsth, dstl, tmvec, w0, eng="vector"):
            if eng == "vector":
                emit(eng, w0,
                     lambda e, i=i: e.tensor_tensor(
                         s_S1[:, 0:1], xb(i, 128, 1), xsb[:, i:i + 1],
                         OP.subtract), inc=None)
                emit(eng, [],
                     lambda e, i=i: e.tensor_tensor(
                         s_S1[:, 1:512], xb(i, 129, 511), xb(i, 128, 511),
                         OP.subtract), inc=None)
                emit(eng, [],
                     lambda e, i=i, tv=tmvec:
                     e.scalar_tensor_tensor(s_S2[:, 0:1], s_S1[:, 0:1],
                                            tv[:, i:i + 1], xsb[:, i:i + 1],
                                            OP.mult, OP.add), inc=None)
                emit(eng, [],
                     lambda e, i=i, tv=tmvec:
                     e.scalar_tensor_tensor(s_S2[:, 1:512], s_S1[:, 1:512],
                                            tv[:, i:i + 1], xb(i, 128, 511),
                                            OP.mult, OP.add), inc=None)
                T2, letter = s_S2, "V"
            else:
                T2, letter = mB(0), "P"
                emit(eng, w0,
                     lambda e, i=i, tv=tmvec: e.tensor_tensor(
                         s_G1[:], xb(i, 128, 512),
                         tv[:, i:i + 1].broadcast_to([128, 512]), OP.mult),
                     inc=None)
                emit(eng, [],
                     lambda e, i=i, T2=T2: e.tensor_tensor(
                         T2[:, 0:1], xsb[:, i:i + 1], sv_tmk1[:, i:i + 1],
                         OP.mult), inc=None)
                emit(eng, [],
                     lambda e, i=i, T2=T2: e.tensor_tensor(
                         T2[:, 1:512], xb(i, 128, 511),
                         sv_tmk1[:, i:i + 1].broadcast_to([128, 511]),
                         OP.mult), inc=None)
                emit(eng, [],
                     lambda e, i=i, T2=T2: e.tensor_tensor(
                         T2[:], T2[:], s_G1[:], OP.add), inc=None)
            emit(eng, [],
                 lambda e, i=i, d=dsth, T2=T2: e.tensor_copy(d(i), T2[:]),
                 inc=None)
            return (letter,
                    emit(eng, [],
                         lambda e, i=i, dh=dsth, dl=dstl, T2=T2:
                         e.tensor_tensor(dl(i), T2[:], dh(i), OP.subtract),
                         inc=letter))

        if V_xk_pre is not None:
            V_xk = V_xk_pre
        else:
            V_xk = []
            for i in range(ND):
                w0 = [mixA_free, ("V", V_ones), xsb_ready] if i == 0 else []
                w0 = w0 + [Lx[i]]
                V_xk.append(mix_tile(i, mAh, mAl, sv_tmk, w0))

        def k_cons(j, bank, tv):
            ev = emit("scalar", [("T", tv), ("V", V_z)],
                      lambda e, j=j, b=bank:
                      e.activation(ek_t(j, 1, 512), pb[b][:], AF.Exp,
                                   accum_out=s_statk[:, j:j + 1]),
                      inc="C")
            return ("C", ev)
        C_ek = proj3(d_wk, mAh, mAl, k_cons, [V_xk[7]],
                     first_prep=[mixB_free],
                     lo_eng="gpsimd",
                     extra_second=[V_xk[-1]])
        T_k_done = n["T"]

        V_xv = []
        for i in range(ND):
            w0 = [mixB_free, xsb_ready] if i == 0 else []
            V_xv.append(mix_tile(i, mBh, mBl, sv_tmv, w0))

        def v_cons(j, bank, tv):
            ev = emit("vector", [("T", tv), C_ek[j]],
                      lambda e, j=j, b=bank:
                      e.scalar_tensor_tensor(ekv_t(j, 1, 512), pb[b][:], 1.0,
                                             ek_t(j, 1, 512), OP.mult, OP.mult,
                                             accum_out=s_statv[:, j:j + 1]),
                      inc="V")
            return ("V", ev)

        V_xr = []

        def mk_rmix(t):
            def th():
                w0 = [("T", T_k_done)] if t == 0 else []
                V_xr.append(mix_tile(t, mAh, mAl, sv_tmr, w0))
            return th
        v_drip = [[mk_rmix(idx // 2)] if idx % 2 == 0 else []
                  for idx in range(32)]
        V_ekv = proj3(d_wv, mBh, mBl, v_cons, [V_xv[-1]],
                      lo_eng="gpsimd", drip=v_drip)

        dma("gpsimd", [C_ek[ND - 1], V_ekv[ND - 1]],
            cc_in[p].ap()[:, 0:16], s_statk[:, 0:16], "G")
        Gv = dma("gpsimd", [],
                 cc_in[p].ap()[:, 16:32], s_statv[:, 0:16], "G")
        stats_free = ("G", Gv)
        Kv = emit("gpsimd", [("G", Gv)],
                  lambda e, ci=cc_in[p].ap(), co=cc_out[p].ap():
                  e.collective_compute(
                      "AllGather", OP.bypass,
                      replica_groups=[list(range(NCORES))],
                      ins=[ci], outs=[co]),
                  inc="K")
        g_last = 0
        for c in range(8):
            g_last = dma("gpsimd", [("K", Kv)] if c == 0 else [],
                         s_g[:, c * 32:(c + 1) * 32], cc_out[p].ap()[c], "G")

        V_wkv = [None] * ND

        def mk_scan(j):
            def th():
                emit("vector", [C_ek[j]],
                     lambda e, j=j: e.tensor_tensor_scan(
                         s_S1[:], s_onescol[:].broadcast_to([128, 512]), ek_t(j, 0, 512),
                         0.0, OP.mult, OP.add), inc=None)
                emit("vector", [],
                     lambda e, j=j: e.scalar_tensor_tensor(
                         ek_t(j, 1, 512), ek_t(j, 1, 512), sv_eu[:, j:j + 1],
                         s_S1[:], OP.mult, OP.add), inc=None)
                emit("vector", [],
                     lambda e, j=j: e.tensor_tensor_scan(
                         s_S2[:], s_onescol[:].broadcast_to([128, 512]), ekv_t(j, 0, 512),
                         0.0, OP.mult, OP.add), inc=None)
                emit("vector", [],
                     lambda e, j=j: e.scalar_tensor_tensor(
                         ekv_t(j, 1, 512), ekv_t(j, 1, 512), sv_eu[:, j:j + 1],
                         s_S2[:], OP.mult, OP.add), inc=None)
            return th

        def mk_offs():
            def th():
                if p == 0:
                    emit("vector", [("G", g_last), ("L", L_consts)],
                         lambda e: e.memset(s_offs[:], 0.0), inc=None)
                    emit("vector", [], lambda e: e.memset(s_tot[:], 0.0),
                         inc=None)
                    for c in range(8):
                        emit("vector", [],
                             lambda e, c=c: e.scalar_tensor_tensor(
                                 s_offs[:], s_g[:, c * 32:(c + 1) * 32],
                                 sv_mask[:, c:c + 1], s_offs[:],
                                 OP.mult, OP.add), inc=None)
                        emit("vector", [],
                             lambda e, c=c: e.scalar_tensor_tensor(
                                 s_tot[:], s_g[:, c * 32:(c + 1) * 32],
                                 1.0, s_tot[:], OP.mult, OP.add), inc=None)
                else:
                    emit("vector", [("G", g_last)],
                         lambda e: e.tensor_copy(s_offs[:], s_tot[:]),
                         inc=None)
                    for c in range(8):
                        emit("vector", [],
                             lambda e, c=c: e.scalar_tensor_tensor(
                                 s_offs[:], s_g[:, c * 32:(c + 1) * 32],
                                 sv_mask[:, c:c + 1], s_offs[:],
                                 OP.mult, OP.add), inc=None)
            return th

        def mk_wkv(j):
            def th():
                emit("vector", [],
                     lambda e, j=j: e.tensor_scalar(
                         ek_t(j, 1, 512), ek_t(j, 1, 512), s_offs[:, j:j + 1],
                         None, OP.add), inc=None)
                emit("vector", [],
                     lambda e, j=j: e.reciprocal(s_S1[:], ek_t(j, 1, 512)),
                     inc=None)
                V_wkv[j] = emit("vector", [],
                                lambda e, j=j: e.scalar_tensor_tensor(
                                    ekv_t(j, 1, 512), ekv_t(j, 1, 512),
                                    s_offs[:, 16 + j:17 + j], s_S1[:],
                                    OP.add, OP.mult), inc="V")
            return th

        T_sq2 = {}
        C_sq = {}

        def mk_stats(j):
            def th():
                C_sq[j] = emit("scalar", [("V", V_wkv[j])],
                               lambda e, j=j:
                               e.activation(ek_t(j, 1, 512), ekv_t(j, 1, 512),
                                            AF.Square),
                               inc="C")
                wS = [("C", C_sq[j])] + ([bank_cons[4]] if j == 0 else [])
                emit("tensor", wS,
                     lambda e, j=j: e.matmul(pb[4][0:1, :], s_onescol[:],
                                             ekv_t(j, 1, 512),
                                             start=(j == 0),
                                             stop=(j == ND - 1)),
                     inc="T" if j == ND - 1 else None)
                wS2 = [("C", C_sq[j])] + ([bank_cons[5]] if j == 0 else [])
                T_sq2[j] = emit("tensor", wS2,
                                lambda e, j=j:
                                e.matmul(pb[5][32:33, :], s_onescol[:],
                                         ek_t(j, 1, 512),
                                         start=(j == 0), stop=(j == ND - 1),
                                         tile_position=(0, 32)),
                                inc="T")
            return th

        r_drip = [[mk_scan(idx)] for idx in range(16)]
        r_drip.append([mk_offs()])
        r_drip += [[mk_wkv(idx - 17)] for idx in range(17, 32)]
        for j in range(11):
            r_drip[20 + j].append(mk_stats(j))

        def r_cons(j, bank, tv):
            ev = emit("scalar", [("T", tv)],
                      lambda e, j=j, b=bank:
                      e.activation(mB(j), pb[b][:], AF.Sigmoid),
                      inc="C")
            return ("C", ev)
        C_r = proj3(d_wr, mAh, mAl, r_cons, [V_xr[-1]],
                    lo_eng="vector", drip=r_drip)
        mk_wkv(ND - 1)()

        for j in range(11, ND):
            mk_stats(j)()
        Vp1 = ln_params(n["T"], 512)

        V_rwkv = [None] * ND
        P_hist = {}

        def d_of(j):
            return mixA[:, (j % 4) * 512:(j % 4) * 512 + 512]

        for j in range(ND):
            bkA = 2 + 2 * (j % 2)
            bkB = 3 + 2 * (j % 2)
            wA = [("V", Vp1), bank_cons[bkA]]
            TbA = emit("tensor", wA,
                       lambda e, j=j, bkA=bkA:
                       e.matmul(pb[bkA][:], sv_lngr[:, j * 128:(j + 1) * 128],
                                s_pb[:, 0:512], start=True, stop=True),
                       inc=None)
            TbB = emit("tensor", [bank_cons[bkB]],
                       lambda e, j=j, bkB=bkB:
                       e.matmul(pb[bkB][:], sv_lngr[:, j * 128:(j + 1) * 128],
                                s_pc[:, 0:512], start=True, stop=True),
                       inc="T")
            d = d_of(j)
            w = [("T", TbB)]
            if j == 0:
                w.append(("C", C_r[ND - 1][1]))
            if j >= 4:
                w.append(("P", P_hist[j - 4]))
            emit("vector", w,
                 lambda e, j=j, d=d, bkA=bkA:
                 e.tensor_tensor(d[:], ekv_t(j, 1, 512), pb[bkA][:], OP.mult),
                 inc=None)
            emit("vector", [],
                 lambda e, d=d, bkB=bkB: e.tensor_tensor(d[:], d[:], pb[bkB][:],
                                                         OP.add), inc=None)
            Vm = emit("vector", [],
                      lambda e, j=j, d=d: e.scalar_tensor_tensor(
                          d[:], d[:], sv_lnb[:, j:j + 1], mB(j),
                          OP.add, OP.mult), inc="V")
            bank_cons[bkA] = ("V", Vm)
            bank_cons[bkB] = ("V", Vm)
            Ch = emit("scalar", [("V", Vm)],
                      lambda e, j=j, d=d: e.copy(rw_h(j), d[:]), inc="C")
            P_hist[j] = emit("gpsimd", [("C", Ch)],
                             lambda e, j=j, d=d:
                             e.tensor_tensor(rw_l(j), d[:], rw_h(j),
                                             OP.subtract), inc="P")
            V_rwkv[j] = ("P", P_hist[j])
        bank_cons[2] = V_rwkv[-1]
        bank_cons[3] = V_rwkv[-1]

        V_x2pre = []

        def wo_cons(j2, bank, tv):
            Vv = emit("vector", [("T", tv)],
                      lambda e, j2=j2, b=bank:
                      e.tensor_tensor(xb(j2, 128, 512), pb[b][:],
                                      xb(j2, 128, 512), OP.add), inc="V")
            V_x2pre.append(Vv)
            return ("V", Vv)
        proj3(d_wo, rw_h, rw_l, wo_cons, [V_rwkv[7]],
              extra_second=[V_rwkv[-1]])

        def prep_w(dram_ap, cols, f, extra):
            Lv = dma("sync", wb_cons[f % 2], wb[f % 2][:, 0:cols], dram_ap, "L")
            Ch = emit("scalar", [("L", Lv), w16_cons[f % 2]] + extra,
                      lambda e, f=f, cols=cols:
                      e.copy(w16[f % 2][:, 0:cols], wb[f % 2][:, 0:cols]),
                      inc="C")
            Vl = emit("vector", [("C", Ch), wbl_cons[f % 2]],
                      lambda e, f=f, cols=cols:
                      e.tensor_tensor(wbl[f % 2][:, 0:cols],
                                      wb[f % 2][:, 0:cols],
                                      w16[f % 2][:, 0:cols], OP.subtract),
                      inc="V")
            wb_cons[f % 2] = [("C", Ch), ("V", Vl)]
            return (Ch, Vl)

        Lw0 = dma("sync", wb_cons[0], wb[0][:, 0:2048], d_w0[0], "L")

        T_sq2 = {}
        for j in range(ND):
            sq = s_S2 if j % 2 == 0 else s_S1
            wq = [("V", V_x2pre[j])]
            if j >= 2:
                wq.append(("T", T_sq2[j - 2]))
            Cs = emit("scalar", wq,
                      lambda e, j=j, sq=sq: e.activation(sq[:], xb(j, 128, 512),
                                                         AF.Square), inc="C")
            wS = [("C", Cs)] + ([bank_cons[4]] if j == 0 else [])
            emit("tensor", wS,
                 lambda e, j=j: e.matmul(pb[4][0:1, :], s_onescol[:],
                                         xb(j, 128, 512),
                                         start=(j == 0), stop=(j == ND - 1)),
                 inc="T" if j == ND - 1 else None)
            wS2 = [("C", Cs)] + ([bank_cons[5]] if j == 0 else [])
            T_sq2[j] = emit("tensor", wS2,
                            lambda e, j=j, sq=sq: e.matmul(pb[5][32:33, :],
                                                           s_onescol[:],
                                                           sq[:], start=(j == 0),
                                                           stop=(j == ND - 1),
                                                           tile_position=(0, 32)),
                            inc="T")
        Vp2 = ln_params(n["T"], 512)
        ChA = emit("scalar", [("L", Lw0), w16_cons[0], V_rwkv[-1], mixB_free],
                   lambda e: e.copy(w16[0][:, 0:1024], wb[0][:, 0:1024]),
                   inc="C")
        ChB = emit("scalar", [],
                   lambda e: e.copy(w16[0][:, 1024:2048], wb[0][:, 1024:2048]),
                   inc="C")
        VlW = emit("vector", [("C", ChB), wbl_cons[0]],
                   lambda e: e.tensor_tensor(wbl[0][:], wb[0][:], w16[0][:],
                                             OP.subtract), inc="V")
        wb_cons[0] = [("C", ChB), ("V", VlW)]
        Tb2 = ln_bcast(Vp2, 512)

        C_xq = {}
        V_xq = {}

        def emit_norm2(j):
            w = [("T", Tb2)] if j == 0 else []
            emit("vector", w,
                 lambda e, j=j: e.tensor_tensor(xb(j, 128, 512), xb(j, 128, 512),
                                                pb[2][:], OP.mult), inc=None)
            Vn = emit("vector", [],
                      lambda e, j=j: e.tensor_tensor(xb(j, 128, 512),
                                                     xb(j, 128, 512),
                                                     pb[3][:], OP.add), inc="V")
            C_xq[j] = emit("scalar", [("V", Vn)],
                           lambda e, j=j: e.copy(xq16_t(j), xb(j, 128, 512)),
                           inc="C")

        def emit_xqb(j):
            V_xq[j] = emit("gpsimd", [("C", C_xq[j])],
                           lambda e, j=j: e.tensor_copy(xqb_t(j), xq16_t(j)),
                           inc="P")

        emit_norm2(0)
        for j in range(1, ND):
            emit_norm2(j)
            emit_xqb(j - 1)
        emit_xqb(ND - 1)
        bank_cons[2] = ("V", n["V"])
        bank_cons[3] = ("V", n["V"])
        T_attn_done = n["T"]

        def prep_hg(f):
            return emit("vector", [("C", C_gelu[f]), hgb_cons[f % 2]],
                        lambda e, f=f: e.tensor_copy(hgb[f % 2][:, 0:512],
                                                     hg16_t(f)),
                        inc="V")

        C_gelu = []
        prepped = {0: (ChA, VlW)}
        for f in range(NF):
            Ch, Vl = prepped[f]
            bank = f % 2
            waits = [("C", Ch), bank_cons[bank]]
            if f == 0:
                waits += [("T", T_attn_done)]
            for i in range(ND):
                w_i = list(waits) if i == 0 else []
                if f == 0:
                    w_i.append(("C", C_xq[i]))
                    if i == 8:
                        w_i.append(("C", ChB))
                emit("tensor", w_i,
                     lambda e, b=bank, f=f, i=i:
                     e.matmul(pb[b][:], w16[f % 2][:, i * 128:(i + 1) * 128],
                              xq16_t(i), start=(i == 0), stop=False),
                     inc=None)
            for i in range(ND):
                w_i = [("V", Vl)] if i == 0 else []
                if f == 0:
                    w_i.append(("P", V_xq[i]))
                lastv = emit("tensor", w_i,
                             lambda e, b=bank, f=f, i=i:
                             e.matmul(pb[b][:], wbl[f % 2][:, i * 128:(i + 1) * 128],
                                      xqb_t(i), start=False, stop=(i == ND - 1)),
                             inc="T" if i == ND - 1 else None)
            w16_cons[f % 2] = ("T", lastv)
            wbl_cons[f % 2] = ("T", lastv)
            if f + 1 < NF:
                prepped[f + 1] = prep_w(d_w0[f + 1], 2048, f + 1,
                                        [V_rwkv[-1]] if f == 0 else [])
            Cg = emit("scalar", [("T", lastv)],
                      lambda e, f=f, b=bank:
                      e.activation(hg16_t(f), pb[b][:], AF.Gelu,
                                   bias=sv_b0[:, f:f + 1]), inc="C")
            C_gelu.append(Cg)
            bank_cons[bank] = ("C", Cg)
            if f == NF - 2:
                w1_early = prep_w(d_w1[0][0], 1024, 0, [])
        T_ffn1 = n["T"]

        V_x3 = {}
        for eh in range(2):
            prepped = {0: (w1_early if eh == 0
                           else prep_w(d_w1[eh][0], 1024, 0, []))}
            hgprep = {0: prep_hg(0)}
            for f in range(NF):
                Ch, Vl = prepped[f]
                Vc = hgprep[f]
                waits = [("C", Ch), ("C", C_gelu[f])]
                for e8 in range(8):
                    w = (waits if e8 == 0 else [])                         + ([bank_cons[e8]] if f == 0 else [])
                    emit("tensor", w,
                         lambda e, f=f, e8=e8:
                         e.matmul(pb[e8][:], w16[f % 2][:, e8 * 128:(e8 + 1) * 128],
                                  hg16_t(f), start=(f == 0), stop=False),
                         inc=None)
                vw = [("V", Vl), ("V", Vc)]
                for e8 in range(8):
                    lastv = emit("tensor", vw if e8 == 0 else [],
                                 lambda e, f=f, e8=e8:
                                 e.matmul(pb[e8][:],
                                          wbl[f % 2][:, e8 * 128:(e8 + 1) * 128],
                                          hgb[f % 2][:, 0:512], start=False,
                                          stop=(f == NF - 1)),
                                 inc="T" if e8 == 7 else None)
                w16_cons[f % 2] = ("T", lastv)
                wbl_cons[f % 2] = ("T", lastv)
                hgb_cons[f % 2] = ("T", lastv)
                if f + 1 < NF:
                    prepped[f + 1] = prep_w(d_w1[eh][f + 1], 1024, f + 1, [])
                    hgprep[f + 1] = prep_hg(f + 1)
            T_close = lastv
            Vv_hist = {}
            for e8 in range(8):
                je = eh * 8 + e8
                w = [("T", T_close)] if e8 == 0 else []
                if e8 >= 2:
                    w.append(("V", Vv_hist[e8 - 2]))
                Cev = emit("scalar", w,
                           lambda e, e8=e8, je=je: e.activation(
                               s_h16[e8 % 2][:, 0:512], pb[e8][:], AF.Identity,
                               bias=sv_b1[:, je:je + 1]), inc="C")
                Vv = emit("vector", [("C", Cev)],
                          lambda e, e8=e8, je=je: e.tensor_tensor(
                              xb(je, 128, 512), s_h16[e8 % 2][:, 0:512],
                              xb(je, 128, 512), OP.add), inc="V")
                Vv_hist[e8] = Vv
                bank_cons[e8] = ("V", Vv)
                V_x3[je] = Vv
        T_ffn2 = n["T"]

        T_sq2 = {}
        for j in range(ND):
            sq = s_S2 if j % 2 == 0 else s_S1
            wq = [("V", V_x3[j])]
            if j >= 2:
                wq.append(("T", T_sq2[j - 2]))
            Cs = emit("scalar", wq,
                      lambda e, j=j, sq=sq:
                      e.activation(sq[:], xb(j, 128, 512), AF.Square),
                      inc="C")
            wS = [("C", Cs)] + ([bank_cons[4]] if j == 0 else [])
            emit("tensor", wS,
                 lambda e, j=j: e.matmul(pb[4][0:1, :], s_onescol[:],
                                         xb(j, 128, 512),
                                         start=(j == 0), stop=(j == ND - 1)),
                 inc="T" if j == ND - 1 else None)
            wS2 = [("C", Cs)] + ([bank_cons[5]] if j == 0 else [])
            T_sq2[j] = emit("tensor", wS2,
                            lambda e, j=j, sq=sq: e.matmul(pb[5][32:33, :],
                                                           s_onescol[:],
                                                           sq[:], start=(j == 0),
                                                           stop=(j == ND - 1),
                                                           tile_position=(0, 32)),
                            inc="T")
        Vp3 = ln_params(n["T"], 512)
        Tb3 = ln_bcast(Vp3, 512)
        V_o = []
        pre_next = None
        if p + 1 < NP:
            pre_next = {"Lx": {}, "V_xk": []}
            Lb1 = dma("sync", [("T", T_ffn1)], xsb[:], d_xb1[p + 1], "L")
            pre_next["xsb"] = ("L", Lb1)

        def pre_mix(i):
            w0 = ([("T", T_ffn1), ("V", V_ones), pre_next["xsb"]]
                  if i == 0 else [])
            w0 = w0 + [pre_next["Lx"][i]]
            pre_next["V_xk"].append(mix_tile(i, mAh, mAl, sv_tmk, w0))

        for j in range(ND):
            w = [("T", Tb3)] if j == 0 else []
            emit("vector", w,
                 lambda e, j=j: e.tensor_tensor(
                     out_t(j), xb(j, 128, 512), pb[2][:], OP.mult),
                 inc=None)
            Vn = emit("vector", [],
                      lambda e, j=j: e.tensor_tensor(
                          out_t(j), out_t(j), pb[3][:], OP.add), inc="V")
            V_o.append(Vn)
            Gs = dma("gpsimd", [("V", Vn)],
                     d_out[j * 128:(j + 1) * 128, p * 512:(p + 1) * 512],
                     out_t(j), "G")
            if pre_next is not None:
                pre_next["Lx"][j] = ("L", dma(
                    "sync", [("V", Vn)], xb(j, 128, 512),
                    d_x[p + 1][j * 128:(j + 1) * 128, :], "L"))
                if j >= 2:
                    pre_mix(j - 2)
        if pre_next is not None:
            pre_mix(ND - 2)
            pre_mix(ND - 1)
        bank_cons[2] = ("V", V_o[-1])
        bank_cons[3] = ("V", V_o[-1])
        mixB_free = ("T", T_ffn2)

        mixA_free = ("T", T_ffn1)
        attn2_free = ("G", n["G"])

    emit("gpsimd", [("G", n["G"]), ("T", n["T"]), ("V", n["V"]),
                    ("C", n["C"]), ("L", n["L"]), ("P", n["P"])],
         lambda e: e.nop(), inc=None)

    with nc.Block() as block:
        @block.sync
        def _(e):
            for f in q["sync"]:
                f(e)

        @block.gpsimd
        def _(e):
            for f in q["gpsimd"]:
                f(e)

        @block.tensor
        def _(e):
            for f in q["tensor"]:
                f(e)

        @block.vector
        def _(e):
            for f in q["vector"]:
                f(e)

        @block.scalar
        def _(e):
            for f in q["scalar"]:
                f(e)

    return nc


_PROG_CACHE = {}


def _get_program():
    if "nc" not in _PROG_CACHE:
        _PROG_CACHE["nc"] = build_program()
    return _PROG_CACHE["nc"]


def _v128(v):
    v = np.asarray(v, dtype=np.float32).reshape(-1)
    return np.ascontiguousarray(v.reshape(-1, 128).T)


def _tile_wt(w):
    wt = np.asarray(w, dtype=np.float32).T
    din, dout = wt.shape
    ni, no = din // 128, dout // 128
    return np.ascontiguousarray(
        wt.reshape(ni, 128, no, 128).transpose(2, 1, 0, 3).reshape(no, 128, din))


def kernel(L0_attn_x, L0_attn_Wk, L0_attn_Wv, L0_attn_Wr, L0_attn_Wo,
           L0_attn_tm_k, L0_attn_tm_v, L0_attn_tm_r, L0_attn_time_decay,
           L0_attn_time_first, L0_attn_ln_g, L0_attn_ln_b,
           L0_ffn_W0, L0_ffn_bias0, L0_ffn_W1, L0_ffn_bias1):
    x2d = np.ascontiguousarray(np.asarray(L0_attn_x, dtype=np.float32)
                               .reshape(8192, 2048))

    w1t = np.asarray(L0_ffn_W1, dtype=np.float32).T
    w1_t = np.ascontiguousarray(
        w1t.reshape(64, 128, 2, 8, 128).transpose(2, 0, 1, 3, 4)
        .reshape(2, 64, 128, 1024))

    common = {
        "wk": _tile_wt(L0_attn_Wk), "wv": _tile_wt(L0_attn_Wv),
        "wr": _tile_wt(L0_attn_Wr), "wo": _tile_wt(L0_attn_Wo),
        "w0": _tile_wt(L0_ffn_W0), "w1": w1_t,
        "tmk": _v128(L0_attn_tm_k), "tmv": _v128(L0_attn_tm_v),
        "tmr": _v128(L0_attn_tm_r), "tf": _v128(L0_attn_time_first),
        "lng": _v128(L0_attn_ln_g), "lnb": _v128(L0_attn_ln_b),
        "lngr": np.ascontiguousarray(np.asarray(L0_attn_ln_g, dtype=np.float32)
                                     .reshape(1, 2048)),
        "b1": _v128(L0_ffn_bias1), "b0": _v128(L0_ffn_bias0),
        "ident": np.eye(128, dtype=np.float32),
    }

    in_maps = []
    for c in range(NCORES):
        m = dict(common)
        for p in range(NP):
            s = p * 4096 + c * 512
            m[f"xin{p}"] = np.ascontiguousarray(x2d[s: s + 512].T)
            if s == 0:
                xb1 = np.zeros((128, 16), np.float32)
            else:
                xb1 = np.ascontiguousarray(x2d[s - 1].reshape(16, 128).T)
            m[f"xb1{p}"] = xb1
        m["mask"] = np.asarray(
            [[1.0 if cc < c else 0.0 for cc in range(8)]], dtype=np.float32)
        in_maps.append(m)

    nc = _get_program()
    res = run_bass_kernel_spmd(nc, in_maps, core_ids=list(range(NCORES)))

    full = np.empty((8192, 2048), dtype=np.float32)
    for c in range(NCORES):
        o = np.asarray(res.results[c]["out"])
        for p in range(NP):
            s = p * 4096 + c * 512
            full[s: s + 512] = o[:, p * 512:(p + 1) * 512].T
    return full.reshape(1, 8192, 2048)
